# version 02
# baseline (speedup 1.0000x reference)
import math
import numpy as np
import ml_dtypes
import concourse.bass as bass
import concourse.mybir as mybir
from concourse.bass_utils import run_bass_kernel_spmd

F32 = mybir.dt.float32
BF16 = mybir.dt.bfloat16
I32 = mybir.dt.int32
AF = mybir.ActivationFunctionType
ALU = mybir.AluOpType
AX = mybir.AxisListType

NB = 2
L = 2048
D = 2048
TOK = NB * L
HD = 512
NH = 4
NAH = 16
AHD = 128
CH = 1024
IF = 8192
NEG = -30000.0
EPS_LN = 1e-5
BASE_LR = 0.01
BASE_LR_INV = float(BASE_LR + np.log(-np.expm1(-BASE_LR)))
NS_COEFFS = [(4.0848, -6.8946, 2.927), (3.9505, -6.3029, 2.6377),
             (3.7418, -5.5913, 2.3037), (2.8769, -3.1427, 1.2046),
             (2.8366, -3.0525, 1.2012)]
INV_SQRT_AHD = 1.0 / math.sqrt(AHD)

CONSTS = ([EPS_LN, 1.0, -1.0, -0.5, 0.5, -128.0, BASE_LR_INV, 1e-7, 1.0 / HD]
          + [a for (a, b, c) in NS_COEFFS] + [c for (a, b, c) in NS_COEFFS])
CIDX = {v: i for i, v in enumerate(CONSTS)}


def build(dbg=(), upto=None):
    nc = bass.Bass("TRN2", target_bir_lowering=False, detect_race_conditions=False)

    def dram(name, shape, dtype, kind="Internal"):
        k = "ExternalOutput" if name in dbg else kind
        return nc.dram_tensor(name, shape, dtype, kind=k).ap()

    x_in = dram("x", [NB, L, D], F32, "ExternalInput")
    wqkv = dram("wqkv", [D, 3 * D], F32, "ExternalInput")
    wsp = dram("wsp", [D, 36], F32, "ExternalInput")
    bs_in = dram("bs", [4], F32, "ExternalInput")
    wo_in = dram("wo", [D, D], F32, "ExternalInput")
    w0_in = dram("w0", [NH, HD, HD], F32, "ExternalInput")
    w1_in = dram("w1", [NH, HD, HD], F32, "ExternalInput")
    w2_in = dram("w2", [NH, HD, HD], F32, "ExternalInput")
    onormg = dram("onormg", [HD], F32, "ExternalInput")
    qkso = dram("qkso", [D, 4], F32, "ExternalInput")
    gw_in = dram("gw", [IF, D], I32, "ExternalInput")
    gsc_in = dram("gsc", [128, IF // 128], F32, "ExternalInput")
    uw_in = dram("uw", [IF, D], I32, "ExternalInput")
    usc_in = dram("usc", [128, IF // 128], F32, "ExternalInput")
    dw_in = dram("dw", [D, IF], I32, "ExternalInput")
    dsc_in = dram("dsc", [D], F32, "ExternalInput")
    ident_in = dram("ident", [128, 128], BF16, "ExternalInput")
    ones_in = dram("ones", [128, 128], BF16, "ExternalInput")
    maskc_in = dram("maskc", [128, 128], F32, "ExternalInput")
    maska_in = dram("maska", [128, 128], F32, "ExternalInput")
    nsdiag_in = dram("nsdiag", [5, 128, 128], BF16, "ExternalInput")
    cvec_in = dram("cvec", [128, len(CONSTS)], F32, "ExternalInput")

    out = dram("out", [NB, L, D], F32, "ExternalOutput")

    xn_hbm = dram("xn_hbm", [TOK, D], BF16)
    qkvT_hbm = dram("qkvT_hbm", [2 * D, TOK], BF16)
    av_hbm = dram("av_hbm", [TOK, D], BF16)
    h_hbm = dram("h_hbm", [IF, TOK], BF16)
    wgb_hbm = dram("wgb_hbm", [IF, D], BF16)
    wub_hbm = dram("wub_hbm", [IF, D], BF16)
    wdb_hbm = dram("wdb_hbm", [D, IF], BF16)
    wob_hbm = dram("wob_hbm", [D, D], BF16)
    tttT_hbm = dram("tttT_hbm", [D, TOK], BF16)
    sT_hbm = dram("sT_hbm", [D, TOK], BF16)
    wup_hbm = dram("wup_hbm", [NB * NH, 4, HD, HD], BF16)

    sem_pool = {}

    def sem(name):
        if name not in sem_pool:
            sem_pool[name] = nc.alloc_semaphore(name)
        return sem_pool[name]

    ident = nc.alloc_sbuf_tensor("ident_sb", [128, 128], BF16)
    ones = nc.alloc_sbuf_tensor("ones_sb", [128, 128], BF16)
    maskc = nc.alloc_sbuf_tensor("maskc_sb", [128, 128], F32)
    maska = nc.alloc_sbuf_tensor("maska_sb", [128, 128], F32)
    nsdiag = nc.alloc_sbuf_tensor("nsdiag_sb", [128, 5 * 128], BF16)
    qks = nc.alloc_sbuf_tensor("qks_sb", [128, 16 * 4], F32)
    lr_tok = nc.alloc_sbuf_tensor("lr_tok_sb", [128, 32 * 12], F32)
    onorm_sb = nc.alloc_sbuf_tensor("onorm_sb", [128, 4], F32)
    bs_sb = nc.alloc_sbuf_tensor("bs_sb", [4, 1], F32)
    cvec = nc.alloc_sbuf_tensor("cvec_sb", [128, len(CONSTS)], F32)

    def cb(val, P=128):
        return cvec[0:P, CIDX[val]:CIDX[val] + 1]

    with nc.Block() as blk:
        @blk.gpsimd
        def _(g):
            g.dma_start(out=ident[:], in_=ident_in).then_inc(sem("c_ld"), 16)
            g.dma_start(out=ones[:], in_=ones_in).then_inc(sem("c_ld"), 16)
            g.dma_start(out=maskc[:], in_=maskc_in).then_inc(sem("c_ld"), 16)
            g.dma_start(out=maska[:], in_=maska_in).then_inc(sem("c_ld"), 16)
            for i in range(5):
                g.dma_start(out=nsdiag[:, i * 128:(i + 1) * 128],
                            in_=nsdiag_in[i]).then_inc(sem("c_ld"), 16)
            for ft in range(16):
                g.dma_start(out=qks[:, ft * 4:(ft + 1) * 4],
                            in_=qkso[ft * 128:(ft + 1) * 128, :]).then_inc(sem("c_ld"), 16)
            for ot in range(4):
                g.dma_start(out=onorm_sb[:, ot:ot + 1],
                            in_=onormg[ot * 128:(ot + 1) * 128][:, None]).then_inc(sem("c_ld"), 16)
            g.dma_start(out=bs_sb[:], in_=bs_in[:, None]).then_inc(sem("c_ld"), 16)
            g.dma_start(out=cvec[:], in_=cvec_in).then_inc(sem("c_ld"), 16)
            g.wait_ge(sem("c_ld"), 16 * (4 + 5 + 16 + 4 + 2))

    _CLD_N = 16 * (4 + 5 + 16 + 4 + 2)
    warm_sb = nc.alloc_sbuf_tensor("warm_sb", [128, 1], F32)
    with nc.Block() as blk:
        @blk.scalar
        def _(sc):
            sc.wait_ge(sem("c_ld"), _CLD_N)
            for fn in (AF.Ln, AF.Exp, AF.Silu, AF.Sigmoid, AF.Abs, AF.Relu, AF.Copy):
                sc.activation(warm_sb[:], cvec[:, 1:2], fn)

        @blk.vector
        def _(v):
            v.wait_ge(sem("c_ld"), _CLD_N)

        @blk.tensor
        def _(t):
            t.wait_ge(sem("c_ld"), _CLD_N)

        @blk.sync
        def _(s):
            s.wait_ge(sem("c_ld"), _CLD_N)

    with (
        nc.sbuf_tensor("a_x", [128, 2 * D], F32) as a_x,
        nc.sbuf_tensor("a_xn", [128, 2 * D], BF16) as a_xn,
        nc.sbuf_tensor("a_junk", [128, D], F32) as a_junk,
        nc.sbuf_tensor("a_st", [128, 2 * 8], F32) as a_st,
        nc.Block() as blk,
    ):
        a_ld = [sem("a_ld0"), sem("a_ld1")]
        a_stq = [sem("a_stq0"), sem("a_stq1")]
        a_v, a_s, a_xd = sem("a_v"), sem("a_s"), sem("a_xd")
        NT = TOK // 128

        @blk.gpsimd
        def _(g):
            for ti in range(NT):
                p, n = ti % 2, ti // 2
                if ti >= 2:
                    g.wait_ge(a_xd, ti - 1)
                b, r = ti // 16, ti % 16
                g.dma_start(out=a_x[:, p * D:(p + 1) * D],
                            in_=x_in[b, r * 128:(r + 1) * 128, :]).then_inc(a_ld[p], 16)

        a_p = sem("a_p")

        @blk.vector
        def _(v):
            for ti in range(NT):
                p, n = ti % 2, ti // 2
                c = p * 8
                xb = a_x[:, p * D:(p + 1) * D]
                v.wait_ge(a_ld[p], 16 * (n + 1))
                v.reduce_sum(a_st[:, c:c + 1], xb, axis=AX.X).then_inc(a_p, 1)
                v.scalar_tensor_tensor(a_junk[:], xb, 1.0, xb,
                                       op0=ALU.bypass, op1=ALU.mult,
                                       accum_out=a_st[:, c + 1:c + 2]).then_inc(a_p, 1)
                v.wait_ge(a_p, 5 * ti + 2)
                v.tensor_scalar_mul(a_st[:, c + 2:c + 3], a_st[:, c:c + 1],
                                    -1.0 / D).then_inc(a_p, 1)
                v.tensor_scalar_mul(a_st[:, c + 3:c + 4], a_st[:, c + 1:c + 2],
                                    1.0 / D).then_inc(a_p, 1)
                v.wait_ge(a_p, 5 * ti + 4)
                v.tensor_mul(a_st[:, c + 4:c + 5], a_st[:, c + 2:c + 3],
                             a_st[:, c + 2:c + 3]).then_inc(a_p, 1)
                v.wait_ge(a_p, 5 * ti + 5)
                v.tensor_sub(a_st[:, c + 5:c + 6], a_st[:, c + 3:c + 4],
                             a_st[:, c + 4:c + 5]).then_inc(a_v, 1)
                v.wait_ge(a_s, 2 * (ti + 1))
                if ti >= 2:
                    v.wait_ge(a_stq[p], 16 * n)
                v.tensor_scalar(a_xn[:, p * D:(p + 1) * D], xb,
                                a_st[:, c + 2:c + 3], a_st[:, c + 7:c + 8],
                                op0=ALU.add, op1=ALU.mult).then_inc(a_xd, 1)

        @blk.scalar
        def _(s):
            for ti in range(NT):
                p = ti % 2
                c = p * 8
                s.wait_ge(a_v, ti + 1)
                s.activation(a_st[:, c + 6:c + 7], a_st[:, c + 5:c + 6], AF.Ln,
                             bias=cb(EPS_LN)).then_inc(a_s, 1)
                s.wait_ge(a_s, 2 * ti + 1)
                s.activation(a_st[:, c + 7:c + 8], a_st[:, c + 6:c + 7], AF.Exp,
                             scale=cb(-0.5)).then_inc(a_s, 1)

        @blk.sync
        def _(s):
            for ti in range(NT):
                p, n = ti % 2, ti // 2
                s.wait_ge(a_xd, ti + 1)
                s.dma_start(out=xn_hbm[ti * 128:(ti + 1) * 128, :],
                            in_=a_xn[:, p * D:(p + 1) * D]).then_inc(a_stq[p], 16)
            s.wait_ge(a_stq[0], 16 * (NT // 2))
            s.wait_ge(a_stq[1], 16 * (NT // 2))

    if upto == 'a':
        return nc, out

    scl4 = nc.alloc_sbuf_tensor("b_scl4", [4, TOK], F32)
    with (
        nc.sbuf_tensor("b_xnT", [128, 16 * L], BF16) as xnT,
        nc.sbuf_tensor("b_wq", [128, 2 * D], BF16) as b_wq,
        nc.sbuf_tensor("b_stage", [128, 2 * D], BF16) as b_stage,
        nc.sbuf_tensor("b_wv", [128, 2 * 16 * 512], BF16) as b_wv,
        nc.sbuf_tensor("b_wsp", [128, 16 * 36], BF16) as b_wsp,
        nc.sbuf_tensor("b_sm", [128, 8 * 12], F32) as b_sm,
    ):
        for b in range(NB):
            with (
                nc.psum_tensor(f"b_pqk{b}", [128, 2 * D], F32) as pqk,
                nc.Block() as blk,
            ):
                bl = [sem(f"b1_ld{b}_0"), sem(f"b1_ld{b}_1")]
                bst = [sem(f"b1_st{b}_0"), sem(f"b1_st{b}_1")]
                bm_, bev = sem(f"b1_mm{b}"), sem(f"b1_ev{b}")
                xl = sem(f"b_xnT{b}")

                @blk.sync
                def _(s):
                    for ft in range(16):
                        s.dma_start(out=xnT[:, ft * L:(ft + 1) * L],
                                    in_=xn_hbm[b * L:(b + 1) * L, ft * 128:(ft + 1) * 128],
                                    transpose=True).then_inc(xl, 16)
                    for j in range(32):
                        p = j % 2
                        s.wait_ge(bev, j + 1)
                        s.dma_start(out=qkvT_hbm[j * 128:(j + 1) * 128, b * L:(b + 1) * L],
                                    in_=b_stage[:, p * D:(p + 1) * D]).then_inc(bst[p], 16)
                    s.wait_ge(bst[0], 16 * 16)
                    s.wait_ge(bst[1], 16 * 16)

                @blk.gpsimd
                def _(g):
                    for j in range(32):
                        p = j % 2
                        if j >= 2:
                            g.wait_ge(bm_, j - 1)
                        g.dma_start(
                            out=b_wq[:, p * D:(p + 1) * D].rearrange(
                                "p (hb j) -> p hb j", j=128),
                            in_=wqkv[:, j * 128:(j + 1) * 128].rearrange(
                                "(hb p) j -> p hb j", p=128)).then_inc(bl[p], 16)

                @blk.tensor
                def _(t):
                    t.wait_ge(xl, 16 * 16)
                    for j in range(32):
                        p = j % 2
                        t.wait_ge(bl[p], 16 * (j // 2 + 1))
                        if j >= 2:
                            t.wait_ge(bev, j - 1)
                        mi = None
                        for h in range(16):
                            for tq in range(4):
                                mi = t.matmul(
                                    pqk[:, p * D + tq * 512: p * D + (tq + 1) * 512],
                                    b_wq[:, p * D + h * 128: p * D + (h + 1) * 128],
                                    xnT[:, h * L + tq * 512: h * L + (tq + 1) * 512],
                                    start=(h == 0), stop=(h == 15))
                        mi.then_inc(bm_, 1)

                @blk.scalar
                def _(s):
                    for j in range(32):
                        p = j % 2
                        s.wait_ge(bm_, j + 1)
                        if j >= 2:
                            s.wait_ge(bst[p], 16 * (j // 2))
                        s.activation(b_stage[:, p * D:(p + 1) * D],
                                     pqk[:, p * D:(p + 1) * D], AF.Copy).then_inc(bev, 1)

            if upto == 'b1':
                continue
            with (
                nc.psum_tensor(f"b_pav{b}", [128, 4 * 512], F32) as pav,
                nc.psum_tensor(f"b_psm{b}", [128, 36], F32) as psm,
                nc.Block() as blk,
            ):
                vl = [sem(f"b2_ld{b}_0"), sem(f"b2_ld{b}_1")]
                vst = [sem(f"b2_st{b}_{i}") for i in range(4)]
                vlw = sem(f"b2_ldw{b}")
                vm, vev = sem(f"b2_mm{b}"), sem(f"b2_ev{b}")
                sm_m, sm_e, sm_a = sem(f"b3_mm{b}"), sem(f"b3_ev{b}"), sem(f"b3_a{b}")

                @blk.gpsimd
                def _(g):
                    for jc in range(4):
                        p = jc % 2
                        if jc >= 2:
                            g.wait_ge(vm, (jc - 1) * 16)
                        g.dma_start(
                            out=b_wv[:, p * 16 * 512:(p + 1) * 16 * 512].rearrange(
                                "p (hb j) -> p hb j", j=512),
                            in_=wqkv[:, 2 * D + jc * 512: 2 * D + (jc + 1) * 512]
                            .rearrange("(hb p) j -> p hb j", p=128)).then_inc(vl[p], 16)
                    g.dma_start(out=b_wsp[:].rearrange("p (hb j) -> p hb j", j=36),
                                in_=wsp.rearrange("(hb p) j -> p hb j", p=128)
                                ).then_inc(vlw, 16)

                @blk.tensor
                def _(t):
                    for jc in range(4):
                        p = jc % 2
                        t.wait_ge(vl[p], 16 * (jc // 2 + 1))
                        for tb in range(16):
                            k = tb % 4
                            it = jc * 16 + tb
                            if it >= 4:
                                t.wait_ge(vev, it - 3)
                            mi = None
                            for h in range(16):
                                mi = t.matmul(
                                    pav[:, k * 512:(k + 1) * 512],
                                    xnT[:, h * L + tb * 128: h * L + (tb + 1) * 128],
                                    b_wv[:, (p * 16 + h) * 512:(p * 16 + h + 1) * 512],
                                    start=(h == 0), stop=(h == 15))
                            mi.then_inc(vm, 1)
                    t.wait_ge(vlw, 16)
                    for tb in range(16):
                        if tb >= 1:
                            t.wait_ge(sm_a, tb)
                        mi = None
                        for h in range(16):
                            mi = t.matmul(psm[:, :],
                                     xnT[:, h * L + tb * 128: h * L + (tb + 1) * 128],
                                     b_wsp[:, h * 36:(h + 1) * 36],
                                     start=(h == 0), stop=(h == 15))
                        mi.then_inc(sm_m, 1)

                @blk.vector
                def _(v):
                    for it in range(64):
                        k = it % 4
                        v.wait_ge(vm, it + 1)
                        if it >= 4:
                            v.wait_ge(vst[k], 16 * (it // 4))
                        v.tensor_copy(b_stage[:, k * 512:(k + 1) * 512],
                                      pav[:, k * 512:(k + 1) * 512]).then_inc(vev, 1)
                    for tb in range(16):
                        co = (b * 16 + tb) * 12
                        v.wait_ge(sm_a, tb + 1)
                        v.tensor_tensor(lr_tok[:, co:co + 12], b_sm[:, 24:36],
                                        b_sm[:, 36:48], op=ALU.add).then_inc(sm_e, 1)

                @blk.sync
                def _(s):
                    for it in range(64):
                        jc, tb = it // 16, it % 16
                        k = it % 4
                        s.wait_ge(vev, it + 1)
                        s.dma_start(out=av_hbm[b * L + tb * 128: b * L + (tb + 1) * 128,
                                               jc * 512:(jc + 1) * 512],
                                    in_=b_stage[:, k * 512:(k + 1) * 512]).then_inc(vst[k], 16)
                    for k in range(4):
                        s.wait_ge(vst[k], 16 * 16)

                @blk.scalar
                def _(s):
                    for tb in range(16):
                        s.wait_ge(sm_m, tb + 1)
                        zz = psm[:, 0:12]
                        if tb >= 1:
                            s.wait_ge(sm_e, tb)
                        s.activation(b_sm[:, 0:12], zz, AF.Abs, bias=cb(BASE_LR_INV))
                        s.activation(b_sm[:, 12:24], b_sm[:, 0:12], AF.Exp, scale=cb(-1.0))
                        s.activation(b_sm[:, 24:36], b_sm[:, 12:24], AF.Ln, bias=cb(1.0))
                        s.activation(b_sm[:, 36:48], zz, AF.Relu,
                                     bias=cb(BASE_LR_INV)).then_inc(sm_a, 1)

            if upto == 'b2':
                continue
            with nc.psum_tensor(f"b_psr{b}", [36, L], F32) as psr:
                with nc.Block() as blk:
                    @blk.tensor
                    def _(t):
                        for h in range(16):
                            for tq in range(4):
                                t.matmul(psr[:, tq * 512:(tq + 1) * 512],
                                         b_wsp[:, h * 36:(h + 1) * 36],
                                         xnT[:, h * L + tq * 512: h * L + (tq + 1) * 512],
                                         start=(h == 0), stop=(h == 15))
                with nc.Block() as blk2:
                    @blk2.scalar
                    def _(s):
                        s.activation(scl4[:, b * L:(b + 1) * L], psr[32:36, :], AF.Silu,
                                     bias=bs_sb[:])

    if upto in ('b1', 'b2'):
        return nc, out

    if upto == 'b3':
        return nc, out

    gsc_sb = nc.alloc_sbuf_tensor("gsc_sb", [128, 64], F32)
    usc_sb = nc.alloc_sbuf_tensor("usc_sb", [128, 64], F32)
    with (
        nc.sbuf_tensor("cp_in", [128, 2 * IF], BF16) as cp_in,
        nc.sbuf_tensor("cp_out", [128, 2 * IF], BF16) as cp_out,
        nc.sbuf_tensor("cp_ds", [128, 2 * 2], F32) as cp_ds,
        nc.Block() as blk,
    ):
        cl = [sem("cp_ld0"), sem("cp_ld1")]
        cst = [sem("cp_st0"), sem("cp_st1")]
        ca = sem("cp_a")
        cv = sem("cp_v")
        jobs = ([("g", gw_in, wgb_hbm, D, i) for i in range(64)]
                + [("u", uw_in, wub_hbm, D, i) for i in range(64)]
                + [("d", dw_in, wdb_hbm, IF, i) for i in range(16)]
                + [("o", wo_in, wob_hbm, D, i) for i in range(16)])

        @blk.gpsimd
        def _(g):
            for n, (kind, src, dst, w, i) in enumerate(jobs):
                p = n % 2
                if n >= 2:
                    g.wait_ge(ca, n - 1)
                g.dma_start(out=cp_in[:, p * IF:p * IF + w],
                            in_=src[i * 128:(i + 1) * 128, :]).then_inc(cl[p], 16)
                if kind == "d":
                    g.dma_start(out=cp_ds[:, 2 * p:2 * p + 1],
                                in_=dsc_in[i * 128:(i + 1) * 128][:, None]
                                ).then_inc(cl[p], 16)
            g.dma_start(out=gsc_sb[:], in_=gsc_in).then_inc(sem("cp_sc"), 16)
            g.dma_start(out=usc_sb[:], in_=usc_in).then_inc(sem("cp_sc"), 16)
            g.wait_ge(sem("cp_sc"), 32)

        @blk.vector
        def _(v):
            for n, (kind, src, dst, w, i) in enumerate(jobs):
                p = n % 2
                v.wait_ge(cl[p], 16 * sum(2 if jobs[m][0] == "d" else 1
                                          for m in range(n + 1) if m % 2 == p))
                if n >= 2:
                    v.wait_ge(cst[p], 16 * (n // 2))
                if kind == "d":
                    v.tensor_scalar_mul(cp_ds[:, 2 * p + 1:2 * p + 2],
                                        cp_ds[:, 2 * p:2 * p + 1], -128.0)
                    v.tensor_scalar(cp_out[:, p * IF:p * IF + w],
                                    cp_in[:, p * IF:p * IF + w],
                                    cp_ds[:, 2 * p:2 * p + 1],
                                    cp_ds[:, 2 * p + 1:2 * p + 2],
                                    op0=ALU.mult, op1=ALU.add).then_inc(ca, 1)
                elif kind == "o":
                    v.tensor_copy(cp_out[:, p * IF:p * IF + w],
                                  cp_in[:, p * IF:p * IF + w]).then_inc(ca, 1)
                else:
                    v.tensor_scalar_add(cp_out[:, p * IF:p * IF + w],
                                        cp_in[:, p * IF:p * IF + w],
                                        -128.0).then_inc(ca, 1)

        @blk.sync
        def _(s):
            for n, (kind, src, dst, w, i) in enumerate(jobs):
                p = n % 2
                s.wait_ge(ca, n + 1)
                s.dma_start(out=dst[i * 128:(i + 1) * 128, :],
                            in_=cp_out[:, p * IF:p * IF + w]).then_inc(cst[p], 16)
            s.wait_ge(cst[0], 16 * (len(jobs) // 2))
            s.wait_ge(cst[1], 16 * (len(jobs) - len(jobs) // 2))

    if upto == 'cpre':
        return nc, out

    with (
        nc.sbuf_tensor("c1_xnT", [128, 16 * L], BF16) as xnT,
        nc.sbuf_tensor("c1_wt", [128, 2 * 16 * 1024], BF16) as c1_wt,
        nc.sbuf_tensor("c1_sg", [128, 8 * L], BF16) as c1_sg,
        nc.sbuf_tensor("c1_hst", [128, 2 * L], BF16) as c1_hst,
    ):
        for b in range(NB):
            with (
                nc.psum_tensor(f"c1_ps{b}", [128, 2 * L], F32) as ps,
                nc.Block() as blk,
            ):
                xl = sem(f"c1_xl{b}")
                wl = [sem(f"c1_wl{b}_0"), sem(f"c1_wl{b}_1")]
                mm = sem(f"c1_mm{b}")
                ev_g = sem(f"c1_evg{b}")
                ev_u = sem(f"c1_evu{b}")
                hst = [sem(f"c1_st{b}_0"), sem(f"c1_st{b}_1")]
                units = [(m, i) for i in range(8) for m in ("g", "u")]

                @blk.sync
                def _(s):
                    for ft in range(16):
                        s.dma_start(out=xnT[:, ft * L:(ft + 1) * L],
                                    in_=xn_hbm[b * L:(b + 1) * L, ft * 128:(ft + 1) * 128],
                                    transpose=True).then_inc(xl, 16)
                    for n, (m, isup) in enumerate(units):
                        p = n % 2
                        src = wgb_hbm if m == "g" else wub_hbm
                        if n >= 2:
                            s.wait_ge(mm, (n - 1) * 8)
                        for h in range(16):
                            s.dma_start(
                                out=c1_wt[:, (p * 16 + h) * 1024:(p * 16 + h + 1) * 1024],
                                in_=src[isup * 1024:(isup + 1) * 1024,
                                        h * 128:(h + 1) * 128],
                                transpose=True).then_inc(wl[p], 16)

                @blk.tensor
                def _(t):
                    t.wait_ge(xl, 256)
                    for n, (m, isup) in enumerate(units):
                        p = n % 2
                        t.wait_ge(wl[p], 16 * 16 * (n // 2 + 1))
                        for isub in range(8):
                            k = isub % 2
                            it = n * 8 + isub
                            if m == "g":
                                if isub >= 2:
                                    t.wait_ge(ev_g, 8 * isup + isub - 1)
                                elif isup >= 1:
                                    t.wait_ge(ev_u, 8 * (isup - 1) + isub + 7)
                            else:
                                if isub >= 2:
                                    t.wait_ge(ev_u, 8 * isup + isub - 1)
                                else:
                                    t.wait_ge(ev_g, 8 * isup + isub + 7)
                            mi = None
                            for h in range(16):
                                for tq in range(4):
                                    mi = t.matmul(
                                        ps[:, k * L + tq * 512: k * L + (tq + 1) * 512],
                                        c1_wt[:, (p * 16 + h) * 1024 + isub * 128:
                                              (p * 16 + h) * 1024 + (isub + 1) * 128],
                                        xnT[:, h * L + tq * 512: h * L + (tq + 1) * 512],
                                        start=(h == 0), stop=(h == 15))
                            mi.then_inc(mm, 1)

                @blk.scalar
                def _(s):
                    for isup in range(8):
                        for isub in range(8):
                            k = isub % 2
                            s.wait_ge(mm, 16 * isup + isub + 1)
                            if isup >= 1:
                                s.wait_ge(ev_u, 8 * (isup - 1) + isub + 1)
                            gi = isup * 8 + isub
                            s.activation(c1_sg[:, isub * L:(isub + 1) * L],
                                         ps[:, k * L:(k + 1) * L], AF.Silu,
                                         scale=gsc_sb[:, gi:gi + 1]).then_inc(ev_g, 1)

                @blk.vector
                def _(v):
                    for isup in range(8):
                        for isub in range(8):
                            k = isub % 2
                            gsi = isup * 8 + isub
                            v.wait_ge(mm, 16 * isup + 8 + isub + 1)
                            v.wait_ge(ev_g, 8 * isup + isub + 1)
                            if gsi >= 2:
                                v.wait_ge(hst[gsi % 2], 16 * (gsi // 2))
                            gi = isup * 8 + isub
                            v.tensor_scalar(c1_hst[:, (gsi % 2) * L:(gsi % 2 + 1) * L],
                                            ps[:, k * L:(k + 1) * L],
                                            usc_sb[:, gi:gi + 1], None,
                                            op0=ALU.mult)
                            v.tensor_mul(c1_hst[:, (gsi % 2) * L:(gsi % 2 + 1) * L],
                                         c1_hst[:, (gsi % 2) * L:(gsi % 2 + 1) * L],
                                         c1_sg[:, isub * L:(isub + 1) * L]).then_inc(ev_u, 1)

                @blk.gpsimd
                def _(g):
                    for isup in range(8):
                        for isub in range(8):
                            gsi = isup * 8 + isub
                            g.wait_ge(ev_u, gsi + 1)
                            gi = isup * 1024 + isub * 128
                            g.dma_start(out=h_hbm[gi:gi + 128, b * L:(b + 1) * L],
                                        in_=c1_hst[:, (gsi % 2) * L:(gsi % 2 + 1) * L]
                                        ).then_inc(hst[gsi % 2], 16)
                    g.wait_ge(hst[0], 16 * 32)
                    g.wait_ge(hst[1], 16 * 32)

    if upto == 'c1':
        return nc, out


    class Ctr:
        def __init__(self, name):
            self.s = sem(name)
            self.n = 0

    c_pe, c_dv, c_ac, c_gp = Ctr("d_pe"), Ctr("d_dv"), Ctr("d_ac"), Ctr("d_gp")
    c_ld, c_st = Ctr("d_ld"), Ctr("d_st")
    CTR = {"t": c_pe, "v": c_dv, "s": c_ac, "g": c_gp}
    DMAC = {"ld": c_ld, "st": c_st}

    def phase(units, extra=None):
        pos = {}
        st0 = c_st.n
        cnt = {k: CTR[k].n for k in CTR}
        dcnt = {k: DMAC[k].n for k in DMAC}
        for u in units:
            name, eng = u[0], u[1]
            if len(u) == 5:
                kind, nd = u[4]
                dcnt[kind] += 16 * nd
                pos[name] = (kind, dcnt[kind])
            else:
                cnt[eng] += 1
                pos[name] = (eng, cnt[eng])

        def section(ek):
            def emit(e):
                if st0 > 0:
                    e.wait_ge(c_st.s, st0)
                for u in units:
                    name, eng, fn, deps = u[0], u[1], u[2], u[3]
                    if eng != ek:
                        continue
                    for dp in deps:
                        if isinstance(dp, tuple):
                            e.wait_ge(DMAC[dp[0]].s, dp[1])
                        else:
                            deng, dval = pos[dp]
                            if deng in DMAC:
                                e.wait_ge(DMAC[deng].s, dval)
                            elif deng != ek:
                                e.wait_ge(CTR[deng].s, dval)
                    ii = fn(e)
                    if len(u) == 4:
                        assert ii is not None, f"unit {name} returned None"
                        ii.then_inc(CTR[ek].s, 1)
                if extra and ek in extra:
                    extra[ek](e)
            return emit

        engs = {u[1] for u in units} | (set(extra) if extra else set())
        with nc.Block() as blk:
            if "t" in engs:
                blk.tensor(section("t"))
            if "v" in engs:
                blk.vector(section("v"))
            if "s" in engs:
                blk.scalar(section("s"))
            if "g" in engs:
                blk.gpsimd(section("g"))
        for k in CTR:
            CTR[k].n = cnt[k]
        for k in DMAC:
            DMAC[k].n = dcnt[k]
        return pos

    with (
        nc.sbuf_tensor("d_B", [128, 11 * 4096], BF16) as dB,
        nc.sbuf_tensor("d_J", [128, 2 * 4096], BF16) as dJ,
        nc.sbuf_tensor("d_BC", [128, 1024], BF16) as dBC,
        nc.sbuf_tensor("d_G", [128, 3 * 2048], F32) as dG,
        nc.sbuf_tensor("d_wnat", [128, 3 * 2048], F32) as dwnat,
        nc.sbuf_tensor("d_w1bf", [128, 2048], BF16) as dw1bf,
        nc.sbuf_tensor("d_w0t", [128, 2048], BF16) as dw0t,
        nc.sbuf_tensor("d_w1t", [128, 2048], BF16) as dw1t,
        nc.sbuf_tensor("d_w2t", [128, 2048], BF16) as dw2t,
        nc.sbuf_tensor("d_nrm", [128, 16], F32) as dnrm,
        nc.sbuf_tensor("d_row", [1, 1024], F32) as drow,
        nc.sbuf_tensor("d_rowb", [1, 1024], BF16) as drowb,
        nc.sbuf_tensor("d_rowb2", [1, 1024], BF16) as drowb2,
        nc.sbuf_tensor("d_sclrow", [1, 1024], F32) as dsclrow,
        nc.sbuf_tensor("d_fr", [128, 16], F32) as dfr,
        nc.sbuf_tensor("d_frb", [128, 16], BF16) as dfrb,
        nc.psum_tensor("d_ps", [128, 4 * 512], F32) as dps,
        nc.psum_tensor("d_pst", [128, 2048], BF16) as dpst,
        nc.psum_tensor("d_psr", [128, 1024], F32) as dpsr,
    ):
        def B(i, half=None):
            if half is None:
                return dB[:, i * 4096:(i + 1) * 4096]
            return dB[:, i * 4096 + half * 2048: i * 4096 + (half + 1) * 2048]

        junk32 = dJ.bitcast(F32)

        def lrcol(b, ct, h, i):
            tb = b * 16 + ct
            return lr_tok[:, tb * 12 + h * 3 + i: tb * 12 + h * 3 + i + 1]

        def emit_transpose_2048(src, dst, tag, ldtgt=None):
            units = []
            for dt in range(4):
                pp = dt % 2
                def pe_fn(t, dt=dt, pp=pp):
                    ii = None
                    for rt in range(4):
                        ii = t.transpose(
                            dpst[:, pp * 1024 + rt * 128: pp * 1024 + (rt + 1) * 128],
                            src[:, rt * 512 + dt * 128: rt * 512 + (dt + 1) * 128],
                            ident[:])
                    return ii
                def ev_fn(v, dt=dt, pp=pp):
                    return v.tensor_copy(dst[:, dt * 512:(dt + 1) * 512],
                                         dpst[:, pp * 1024:pp * 1024 + 512])
                deps = ([f"{tag}e{dt - 2}"] if dt >= 2 else [])
                if ldtgt is not None:
                    deps = deps + [("ld", ldtgt)]
                units.append((f"{tag}t{dt}", "t", pe_fn, deps))
                units.append((f"{tag}e{dt}", "v", ev_fn, [f"{tag}t{dt}"]))
            phase(units)

        def emit_fm2tok(src, dst, tag, scale_fns=None):
            units = []
            for ct in range(8):
                pp = ct % 2
                def pe_fn(t, ct=ct, pp=pp):
                    ii = None
                    for dt in range(4):
                        ii = t.transpose(
                            dpst[:, pp * 1024 + dt * 128: pp * 1024 + (dt + 1) * 128],
                            src[:, dt * CH + ct * 128: dt * CH + (ct + 1) * 128],
                            ident[:])
                    return ii
                def ev_fn(v, ct=ct, pp=pp):
                    if scale_fns is None:
                        return v.tensor_copy(dst[:, ct * 512:(ct + 1) * 512],
                                             dpst[:, pp * 1024:pp * 1024 + 512])
                    return v.tensor_scalar_mul(dst[:, ct * 512:(ct + 1) * 512],
                                               dpst[:, pp * 1024:pp * 1024 + 512],
                                               scale_fns(ct))
                deps = ([f"{tag}e{ct - 2}"] if ct >= 2 else [])
                units.append((f"{tag}t{ct}", "t", pe_fn, deps))
                units.append((f"{tag}e{ct}", "v", ev_fn, [f"{tag}t{ct}"]))
            phase(units)

        def emit_tok2fm(src, dst, tag):
            units = []
            for dt in range(4):
                pp = dt % 2
                def pe_fn(t, dt=dt, pp=pp):
                    ii = None
                    for ct in range(8):
                        ii = t.transpose(
                            dpst[:, pp * 1024 + ct * 128: pp * 1024 + (ct + 1) * 128],
                            src[:, ct * 512 + dt * 128: ct * 512 + (dt + 1) * 128],
                            ident[:])
                    return ii
                def ev_fn(v, dt=dt, pp=pp):
                    return v.tensor_copy(dst[:, dt * CH:(dt + 1) * CH],
                                         dpst[:, pp * 1024:(pp + 1) * 1024])
                deps = ([f"{tag}e{dt - 2}"] if dt >= 2 else [])
                units.append((f"{tag}t{dt}", "t", pe_fn, deps))
                units.append((f"{tag}e{dt}", "v", ev_fn, [f"{tag}t{dt}"]))
            phase(units)

        def emit_head_prep(h):
            ld0 = c_ld.n

            def g_fn(g):
                for j, srcw in enumerate((w0_in, w1_in, w2_in)):
                    g.dma_start(
                        out=dwnat[:, j * 2048:(j + 1) * 2048].rearrange(
                            "p (r d) -> p r d", d=512),
                        in_=srcw[h].rearrange("(r p) d -> p r d", p=128)
                    ).then_inc(c_ld.s, 16)
                    dstb = dw1bf[:] if j == 1 else B(0, j // 2)
                    g.dma_start(
                        out=dstb.rearrange("p (r d) -> p r d", d=512),
                        in_=srcw[h].rearrange("(r p) d -> p r d", p=128)
                    ).then_inc(c_ld.s, 16)

            def n_fn(v):
                ii = None
                for j in range(3):
                    for r in range(4):
                        ii = v.scalar_tensor_tensor(
                            junk32[:, 0:512],
                            dwnat[:, j * 2048 + r * 512: j * 2048 + (r + 1) * 512],
                            1.0,
                            dwnat[:, j * 2048 + r * 512: j * 2048 + (r + 1) * 512],
                            op0=ALU.bypass, op1=ALU.mult,
                            accum_out=dnrm[:, j * 4 + r: j * 4 + r + 1])
                return ii

            def sq_fn(s):
                s.activation(dnrm[:, 0:12], dnrm[:, 0:12], AF.Ln)
                return s.activation(dnrm[:, 0:12], dnrm[:, 0:12], AF.Exp,
                                    scale=cb(0.5))

            phase([
                ("hwl", "g", g_fn, [], ("ld", 6)),
                ("hwn", "v", n_fn, [("ld", ld0 + 96)]),
                ("hws", "s", sq_fn, ["hwn"]),
            ])
            emit_transpose_2048(B(0, 0), dw0t[:], "w0t", ldtgt=ld0 + 96)
            emit_transpose_2048(B(0, 1), dw2t[:], "w2t", ldtgt=ld0 + 96)
            emit_transpose_2048(dw1bf[:], dw1t[:], "w1t", ldtgt=ld0 + 96)

        def emit_loads(b, h, c, with_kv):
            nd = 4 if with_kv else 2
            ld0 = c_ld.n

            def g_fn(g):
                cc = b * L + c * CH
                g.dma_start(out=B(0).rearrange("p (a c) -> p a c", c=CH),
                            in_=qkvT_hbm[h * 512:(h + 1) * 512, cc:cc + CH]
                            .rearrange("(a p) c -> p a c", p=128)).then_inc(c_ld.s, 16)
                if with_kv:
                    g.dma_start(out=B(1).rearrange("p (a c) -> p a c", c=CH),
                                in_=qkvT_hbm[D + h * 512: D + (h + 1) * 512, cc:cc + CH]
                                .rearrange("(a p) c -> p a c", p=128)).then_inc(c_ld.s, 16)
                    g.dma_start(out=B(2).rearrange("p (a c) -> p a c", c=512),
                                in_=av_hbm[cc:cc + CH, h * 512:(h + 1) * 512]
                                .rearrange("(a p) c -> p a c", p=128)).then_inc(c_ld.s, 16)
                g.dma_start(out=dsclrow[:],
                            in_=scl4[h:h + 1, cc:cc + CH]).then_inc(c_ld.s, 16)

            phase([("ldc", "g", g_fn, [], ("ld", nd))])
            return ld0 + 16 * nd

        def emit_norm_prologue(src_i, dst_i, tag, ldtgt):
            def silu_fn(s):
                return s.activation(B(src_i), B(src_i), AF.Silu)

            def sq_fn(v):
                return v.tensor_mul(dJ[:, 0:4096], B(src_i), B(src_i))

            def ssq_fn(t):
                ii = None
                for cq in range(2):
                    for dt in range(4):
                        ii = t.matmul(dpsr[0:1, cq * 512:(cq + 1) * 512],
                                      ones[:, 0:1],
                                      dJ[:, dt * CH + cq * 512: dt * CH + (cq + 1) * 512],
                                      start=(dt == 0), stop=(dt == 3))
                return ii

            def rowv_fn(v):
                return v.tensor_copy(drow[:], dpsr[0:1, 0:CH])

            def rows_fn(s):
                s.activation(drow[:], drow[:], AF.Ln)
                return s.activation(drow[:], drow[:], AF.Exp, scale=cb(0.5, 1))

            def rowv2_fn(v):
                return v.tensor_copy(drowb[:], drow[:])

            def bc_fn(t):
                ii = None
                for cq in range(2):
                    ii = t.matmul(dpsr[:, cq * 512:(cq + 1) * 512], ones[0:1, :],
                                  drowb[0:1, cq * 512:(cq + 1) * 512],
                                  start=True, stop=True)
                return ii

            def bce_fn(v):
                v.tensor_scalar_add(junk32[:, 0:CH], dpsr[:, 0:CH], 1e-5)
                v.reciprocal(junk32[:, 0:CH], junk32[:, 0:CH])
                return v.tensor_copy(dBC[:], junk32[:, 0:CH])

            def mul_fn(v):
                ii = None
                for dt in range(4):
                    ii = v.tensor_mul(B(dst_i)[:, dt * CH:(dt + 1) * CH],
                                      B(src_i)[:, dt * CH:(dt + 1) * CH],
                                      dBC[:])
                return ii

            phase([
                (f"{tag}si", "s", silu_fn, [("ld", ldtgt)]),
                (f"{tag}sq", "v", sq_fn, [f"{tag}si"]),
                (f"{tag}ss", "t", ssq_fn, [f"{tag}sq"]),
                (f"{tag}r1", "v", rowv_fn, [f"{tag}ss"]),
                (f"{tag}rs", "s", rows_fn, [f"{tag}r1"]),
                (f"{tag}r2", "v", rowv2_fn, [f"{tag}rs"]),
                (f"{tag}bc", "t", bc_fn, [f"{tag}r2"]),
                (f"{tag}be", "v", bce_fn, [f"{tag}bc"]),
                (f"{tag}mu", "v", mul_fn, [f"{tag}be"]),
            ])

        def mm_fm(t, pair, statT, rhs, ot, nk=4):
            ii = None
            for dt in range(nk):
                for cq in range(2):
                    ii = t.matmul(
                        dps[:, pair * 1024 + cq * 512: pair * 1024 + (cq + 1) * 512],
                        statT[:, dt * 512 + ot * 128: dt * 512 + (ot + 1) * 128],
                        rhs[:, dt * CH + cq * 512: dt * CH + (cq + 1) * 512],
                        start=(dt == 0), stop=(dt == nk - 1))
            return ii

        def emit_forward(b, h, c, ldw=None):
            units = []
            for ot in range(4):
                pair = ot % 2
                units.append((f"fh{ot}", "t",
                              lambda t, ot=ot, pair=pair: mm_fm(t, pair, dw2t[:], B(3), ot),
                              ([f"fhe{ot - 2}"] if ot >= 2 else [])
                              + ([("ld", ldw)] if ldw is not None and ot == 0 else [])))
                units.append((f"fhe{ot}", "v",
                              lambda v, ot=ot, pair=pair: v.tensor_copy(
                                  B(6)[:, ot * CH:(ot + 1) * CH],
                                  dps[:, pair * 1024:(pair + 1) * 1024]),
                              [f"fh{ot}"]))
            for ot in range(4):
                pair = ot % 2
                units.append((f"fg{ot}", "t",
                              lambda t, ot=ot, pair=pair: mm_fm(t, pair, dw0t[:], B(3), ot),
                              [f"fge{ot - 2}"] if ot >= 2 else [f"fhe3"]))
                units.append((f"fge{ot}", "s",
                              lambda s, ot=ot, pair=pair: s.activation(
                                  B(7)[:, ot * CH:(ot + 1) * CH],
                                  dps[:, pair * 1024:(pair + 1) * 1024], AF.Silu),
                              [f"fg{ot}"]))
                units.append((f"fgh{ot}", "v",
                              lambda v, ot=ot: v.tensor_mul(
                                  B(6)[:, ot * CH:(ot + 1) * CH],
                                  B(7)[:, ot * CH:(ot + 1) * CH],
                                  B(6)[:, ot * CH:(ot + 1) * CH]),
                              [f"fge{ot}"]))
            for ot in range(4):
                pair = ot % 2
                units.append((f"fo{ot}", "t",
                              lambda t, ot=ot, pair=pair: mm_fm(t, pair, dw1t[:], B(6), ot),
                              ([f"foe{ot - 2}"] if ot >= 2 else [f"fge3"]) + ["fgh3"]))
                units.append((f"foe{ot}", "v",
                              lambda v, ot=ot, pair=pair: v.tensor_copy(
                                  B(7)[:, ot * CH:(ot + 1) * CH],
                                  dps[:, pair * 1024:(pair + 1) * 1024]),
                              [f"fo{ot}", "fgh3"]))
            phase(units)

        def emit_rms_store(b, h, c, ldtgt):
            def sq_fn(v):
                return v.tensor_mul(dJ[:, 0:4096], B(7), B(7))

            def ssq_fn(t):
                ii = None
                for cq in range(2):
                    for dt in range(4):
                        ii = t.matmul(dpsr[0:1, cq * 512:(cq + 1) * 512],
                                      ones[:, 0:1],
                                      dJ[:, dt * CH + cq * 512: dt * CH + (cq + 1) * 512],
                                      start=(dt == 0), stop=(dt == 3))
                return ii

            def rowv_fn(v):
                return v.tensor_copy(drow[:], dpsr[0:1, 0:CH])

            def rows_fn(s):
                s.activation(drow[:], drow[:], AF.Ln, scale=cb(1.0 / HD, 1),
                             bias=cb(EPS_LN, 1))
                return s.activation(drow[:], drow[:], AF.Exp, scale=cb(-0.5, 1))

            def rowv2_fn(v):
                v.tensor_copy(drowb[:], drow[:])
                return v.tensor_copy(drowb2[:], dsclrow[:])

            def bc_fn(t):
                ii = None
                for cq in range(2):
                    t.matmul(dpsr[:, cq * 512:(cq + 1) * 512], ones[0:1, :],
                             drowb[0:1, cq * 512:(cq + 1) * 512],
                             start=True, stop=True)
                    ii = t.matmul(dps[:, cq * 512:(cq + 1) * 512], ones[0:1, :],
                                  drowb2[0:1, cq * 512:(cq + 1) * 512],
                                  start=True, stop=True)
                return ii

            def bce_fn(v):
                v.tensor_copy(junk32[:, CH:2 * CH], dps[:, 0:CH])
                v.tensor_mul(junk32[:, 0:CH], junk32[:, CH:2 * CH], dpsr[:, 0:CH])
                return v.tensor_copy(dBC[:], junk32[:, 0:CH])

            def fin_fn(v):
                ii = None
                for ot in range(4):
                    ii = v.scalar_tensor_tensor(
                        B(6)[:, ot * CH:(ot + 1) * CH],
                        B(7)[:, ot * CH:(ot + 1) * CH],
                        onorm_sb[:, ot:ot + 1],
                        dBC[:],
                        op0=ALU.mult, op1=ALU.mult)
                return ii

            def st_fn(g):
                cc = b * L + c * CH
                g.dma_start(out=tttT_hbm[h * 512:(h + 1) * 512, cc:cc + CH]
                            .rearrange("(a p) c -> p a c", p=128),
                            in_=B(6).rearrange("p (a c) -> p a c", c=CH)
                            ).then_inc(c_st.s, 16)

            phase([
                ("esq", "v", sq_fn, []),
                ("ess", "t", ssq_fn, ["esq"]),
                ("er1", "v", rowv_fn, ["ess"]),
                ("ers", "s", rows_fn, ["er1"]),
                ("er2", "v", rowv2_fn, ["ers", ("ld", ldtgt)]),
                ("ebc", "t", bc_fn, ["er2"]),
                ("ebe", "v", bce_fn, ["ebc"]),
                ("efi", "v", fin_fn, ["ebe"]),
                ("est", "g", st_fn, ["efi"], ("st", 1)),
            ])

        def emit_backward(b, h):
            units = []
            for ot in range(4):
                pair = ot % 2
                units.append((f"bg{ot}", "t",
                              lambda t, ot=ot, pair=pair: mm_fm(t, pair, dw0t[:], B(4), ot),
                              [f"bge{ot - 2}"] if ot >= 2 else []))
                units.append((f"bge{ot}", "v",
                              lambda v, ot=ot, pair=pair: v.tensor_copy(
                                  B(6)[:, ot * CH:(ot + 1) * CH],
                                  dps[:, pair * 1024:(pair + 1) * 1024]),
                              [f"bg{ot}"]))
            for ot in range(4):
                pair = ot % 2
                units.append((f"bh{ot}", "t",
                              lambda t, ot=ot, pair=pair: mm_fm(t, pair, dw2t[:], B(4), ot),
                              [f"bhe{ot - 2}"] if ot >= 2 else ["bge3"]))
                units.append((f"bhe{ot}", "v",
                              lambda v, ot=ot, pair=pair: v.tensor_copy(
                                  B(7)[:, ot * CH:(ot + 1) * CH],
                                  dps[:, pair * 1024:(pair + 1) * 1024]),
                              [f"bh{ot}"]))
            for ht in range(4):
                pair = ht % 2
                units.append((f"bd{ht}", "t",
                              lambda t, ht=ht, pair=pair: mm_fm(t, pair, dw1bf[:], B(5), ht),
                              [f"bde{ht - 2}"] if ht >= 2 else ["bhe3"]))
                units.append((f"bde{ht}", "v",
                              lambda v, ht=ht, pair=pair: v.tensor_copy(
                                  B(4)[:, ht * CH:(ht + 1) * CH],
                                  dps[:, pair * 1024:(pair + 1) * 1024]),
                              [f"bd{ht}", "bhe3"]))
            phase(units)

            def sg_fn(s):
                return s.activation(B(8), B(6), AF.Silu)

            def sig_fn(s):
                return s.activation(B(3), B(6), AF.Sigmoid)

            def ew_fn(v):
                v.tensor_mul(B(9), B(8), B(7))
                v.tensor_mul(B(10), B(4), B(8))
                v.tensor_sub(B(6), B(6), B(8))
                v.tensor_scalar_add(B(6), B(6), 1.0)
                v.tensor_mul(B(7), B(4), B(7))
                v.tensor_mul(B(7), B(7), B(3))
                return v.tensor_mul(B(6), B(7), B(6))
            phase([
                ("xsg", "s", sg_fn, []),
                ("xsig", "s", sig_fn, []),
                ("xew", "v", ew_fn, ["xsg", "xsig"]),
            ])

        def emit_dw(b, h, tag, stat_i, rhs_i, gslot):
            units = []
            for ot in range(4):
                bank = ot % 4
                def pe_fn(t, ot=ot, bank=bank):
                    ii = None
                    for ct in range(8):
                        ii = t.matmul(
                            dps[:, bank * 512:(bank + 1) * 512],
                            B(stat_i)[:, ct * 512 + ot * 128: ct * 512 + (ot + 1) * 128],
                            B(rhs_i)[:, ct * 512:(ct + 1) * 512],
                            start=(ct == 0), stop=(ct == 7))
                    return ii

                def ev_fn(v, ot=ot, bank=bank):
                    v.tensor_copy(dG[:, gslot * 2048 + ot * 512: gslot * 2048 + (ot + 1) * 512],
                                  dps[:, bank * 512:(bank + 1) * 512])
                    return v.scalar_tensor_tensor(
                        junk32[:, 0:512],
                        dG[:, gslot * 2048 + ot * 512: gslot * 2048 + (ot + 1) * 512],
                        1.0,
                        dG[:, gslot * 2048 + ot * 512: gslot * 2048 + (ot + 1) * 512],
                        op0=ALU.bypass, op1=ALU.mult,
                        accum_out=dfr[:, gslot * 4 + ot: gslot * 4 + ot + 1])

                units.append((f"{tag}m{ot}", "t", pe_fn,
                              [f"{tag}e{ot - 2}"] if ot >= 2 else []))
                units.append((f"{tag}e{ot}", "v", ev_fn, [f"{tag}m{ot}"]))
            phase(units)

        def nsx(m, pp):
            return B(6 + m, pp)

        def nsxt(m):
            return (B(9, 0), B(9, 1), B(10, 0))[m]

        def nsa(m):
            return (B(10, 1), B(2, 0), B(2, 1))[m]

        def nsy(m):
            return (B(5, 0), B(5, 1), B(4, 0))[m]

        def emit_x_prep(bh):
            def cast_fn(v):
                return v.tensor_copy(dfrb[:, 0:12], dfr[:, 0:12])

            def fsum_fn(t):
                ii = None
                for m in range(3):
                    ii = t.matmul(dpsr[0:1, m * 4:(m + 1) * 4], ones[:, 0:1],
                                  dfrb[:, m * 4:(m + 1) * 4], start=True, stop=True)
                return ii

            def row_fn(v):
                ii = None
                for m in range(3):
                    ii = v.reduce_sum(drow[:, m:m + 1], dpsr[0:1, m * 4:(m + 1) * 4],
                                      axis=AX.X)
                return ii

            def rowcp_fn(v):
                return v.tensor_copy(drowb[:, 0:4], drow[:, 0:4])

            def bc_fn(t):
                return t.matmul(dpsr[:, 512:516], ones[0:1, :],
                                drowb[0:1, 0:4], start=True, stop=True)

            def bce_fn(v):
                return v.tensor_copy(dfr[:, 12:15], dpsr[:, 512:515])

            def bcs_fn(s):
                s.activation(dfr[:, 12:15], dfr[:, 12:15], AF.Ln)
                return s.activation(dfr[:, 12:15], dfr[:, 12:15], AF.Exp,
                                    scale=cb(0.5))

            def bcv_fn(v):
                return v.tensor_scalar_add(dfr[:, 12:15], dfr[:, 12:15], 1e-7)

            def bcs2_fn(s):
                s.activation(dfr[:, 12:15], dfr[:, 12:15], AF.Ln)
                return s.activation(dfr[:, 12:15], dfr[:, 12:15], AF.Exp,
                                    scale=cb(-1.0))

            def x_fn(v):
                ii = None
                for m in range(3):
                    for hh in range(2):
                        ii = v.tensor_scalar_mul(
                            B(6 + m, 0)[:, hh * 1024:(hh + 1) * 1024],
                            dG[:, m * 2048 + hh * 1024: m * 2048 + (hh + 1) * 1024],
                            dfr[:, 12 + m: 13 + m])
                return ii

            phase([
                ("xc", "v", cast_fn, []),
                ("xf", "t", fsum_fn, ["xc"]),
                ("xr", "v", row_fn, ["xf"]),
            ])
            phase([("xcp", "v", rowcp_fn, [])])
            phase([
                ("xb", "t", bc_fn, []),
                ("xbe", "v", bce_fn, ["xb"]),
                ("xbs", "s", bcs_fn, ["xbe"]),
                ("xbv", "v", bcv_fn, ["xbs"]),
                ("xbs2", "s", bcs2_fn, ["xbv"]),
                ("xx", "v", x_fn, ["xbs2"]),
            ])
            for m in range(3):
                emit_transpose_2048(nsx(m, 0), nsxt(m), f"xt{m}")

        def emit_ns(bh):
            units = []
            last_ev = {}

            def mm4(t, bank, stat, rhs, it):
                ii = None
                for jt in range(4):
                    ii = t.matmul(
                        dps[:, bank * 512:(bank + 1) * 512],
                        stat[:, jt * 512 + it * 128: jt * 512 + (it + 1) * 128],
                        rhs[:, jt * 512:(jt + 1) * 512],
                        start=(jt == 0), stop=(jt == 3))
                return ii

            udps = [0]
            udpst = [0]
            bank_unit = {}
            tbank_unit = {}

            def add_unit(tag, pe_fn, ev_fn, deps_pe, kind="mm"):
                deps = list(deps_pe)
                if kind == "mm":
                    bank = udps[0] % 4
                    udps[0] += 1
                    if bank in bank_unit:
                        deps.append(bank_unit[bank] + "e")
                    bank_unit[bank] = tag
                else:
                    bank = udpst[0] % 2
                    udpst[0] += 1
                    if bank in tbank_unit:
                        deps.append(tbank_unit[bank] + "e")
                    tbank_unit[bank] = tag
                units.append((tag + "m", "t",
                              (lambda t, pe_fn=pe_fn, bank=bank: pe_fn(t, bank)), deps))
                units.append((tag + "e", "v",
                              (lambda v, ev_fn=ev_fn, bank=bank: ev_fn(v, bank)),
                              [tag + "m"]))
                last_ev[tag] = tag + "e"

            for i in range(5):
                pp = i % 2
                ai, ci = NS_COEFFS[i][0], NS_COEFFS[i][2]
                for m in range(3):
                    xt = nsxt(m)
                    for it in range(4):
                        dep = [f"i{i-1}m{m}xt{3}e"] if i >= 1 else []
                        add_unit(f"i{i}m{m}a{it}",
                                 lambda t, bank, m=m, it=it, xt=xt: mm4(t, bank, xt, xt, it),
                                 lambda v, bank, m=m, it=it: v.tensor_copy(
                                     nsa(m)[:, it * 512:(it + 1) * 512],
                                     dps[:, bank * 512:(bank + 1) * 512]),
                                 dep)
                for m in range(3):
                    for it in range(4):
                        def ypf(t, bank, m=m, it=it, pp=pp):
                            ii = None
                            for jt in range(4):
                                ii = t.matmul(
                                    dps[:, bank * 512:(bank + 1) * 512],
                                    nsa(m)[:, jt * 512 + it * 128: jt * 512 + (it + 1) * 128],
                                    nsx(m, pp)[:, jt * 512:(jt + 1) * 512],
                                    start=(jt == 0), stop=(jt == 3))
                            return ii
                        add_unit(f"i{i}m{m}y{it}",
                                 ypf,
                                 lambda v, bank, m=m, it=it, ci=ci: v.tensor_scalar_mul(
                                     nsy(m)[:, it * 512:(it + 1) * 512],
                                     dps[:, bank * 512:(bank + 1) * 512], ci),
                                 [f"i{i}m{m}a{3}e"])
                for m in range(3):
                    for it in range(4):
                        def zpf(t, bank, m=m, it=it, i=i):
                            for jt in range(4):
                                t.matmul(
                                    dps[:, bank * 512:(bank + 1) * 512],
                                    nsa(m)[:, jt * 512 + it * 128: jt * 512 + (it + 1) * 128],
                                    nsy(m)[:, jt * 512:(jt + 1) * 512],
                                    start=(jt == 0), stop=False)
                            return t.matmul(
                                dps[:, bank * 512:(bank + 1) * 512],
                                nsdiag[:, i * 128:(i + 1) * 128],
                                nsy(m)[:, it * 512:(it + 1) * 512],
                                start=False, stop=True)
                        add_unit(f"i{i}m{m}z{it}",
                                 zpf,
                                 lambda v, bank, m=m, it=it, pp=pp, ai=ai: v.scalar_tensor_tensor(
                                     nsx(m, 1 - pp)[:, it * 512:(it + 1) * 512],
                                     nsx(m, pp)[:, it * 512:(it + 1) * 512],
                                     ai,
                                     dps[:, bank * 512:(bank + 1) * 512],
                                     op0=ALU.mult, op1=ALU.add),
                                 [f"i{i}m{m}y{3}e"])
                if i < 4:
                    for m in range(3):
                        for dt in range(4):
                            def tpf(t, bank, m=m, dt=dt, pp=pp):
                                ii = None
                                for rt in range(4):
                                    ii = t.transpose(
                                        dpst[:, bank * 1024 + rt * 128:
                                             bank * 1024 + (rt + 1) * 128],
                                        nsx(m, 1 - pp)[:, rt * 512 + dt * 128:
                                                       rt * 512 + (dt + 1) * 128],
                                        ident[:])
                                return ii
                            def tef(v, bank, m=m, dt=dt):
                                return v.tensor_copy(
                                    nsxt(m)[:, dt * 512:(dt + 1) * 512],
                                    dpst[:, bank * 1024:bank * 1024 + 512])
                            add_unit(f"i{i}m{m}xt{dt}", tpf, tef,
                                     [f"i{i}m{m}z{3}e"], kind="tp")
            phase(units)

        def emit_renorm_store(bh):
            for m in range(3):
                pp5 = 5 % 2

                def t_fn(v, m=m):
                    ii = None
                    for hh in range(2):
                        v.tensor_add(dG[:, m * 2048 + hh * 1024: m * 2048 + (hh + 1) * 1024],
                                     dwnat[:, m * 2048 + hh * 1024: m * 2048 + (hh + 1) * 1024],
                                     nsx(m, 1)[:, hh * 1024:(hh + 1) * 1024])
                    for r in range(4):
                        ii = v.scalar_tensor_tensor(
                            junk32[:, 0:512],
                            dG[:, m * 2048 + r * 512: m * 2048 + (r + 1) * 512],
                            1.0,
                            dG[:, m * 2048 + r * 512: m * 2048 + (r + 1) * 512],
                            op0=ALU.bypass, op1=ALU.mult,
                            accum_out=dfr[:, 4 + r: 5 + r])
                    return ii

                def rs_fn(s, m=m):
                    s.activation(dfr[:, 4:8], dfr[:, 4:8], AF.Ln)
                    return s.activation(dfr[:, 4:8], dfr[:, 4:8], AF.Exp,
                                        scale=cb(0.5))

                def rv1_fn(v, m=m):
                    return v.tensor_scalar_add(dfr[:, 4:8], dfr[:, 4:8], 1e-5)

                def rs2_fn(s, m=m):
                    s.activation(dfr[:, 4:8], dfr[:, 4:8], AF.Ln)
                    return s.activation(dfr[:, 4:8], dfr[:, 4:8], AF.Exp,
                                        scale=cb(-1.0))

                def rv_fn(v, m=m):
                    v.tensor_mul(dfr[:, 4:8], dfr[:, 4:8], dnrm[:, m * 4:(m + 1) * 4])
                    ii = None
                    for r in range(4):
                        ii = v.tensor_scalar_mul(
                            B(0, 0)[:, r * 512:(r + 1) * 512],
                            dG[:, m * 2048 + r * 512: m * 2048 + (r + 1) * 512],
                            dfr[:, 4 + r: 5 + r])
                    return ii

                phase([
                    (f"rn{m}t", "v", t_fn, []),
                    (f"rn{m}s", "s", rs_fn, [f"rn{m}t"]),
                    (f"rn{m}v1", "v", rv1_fn, [f"rn{m}s"]),
                    (f"rn{m}s2", "s", rs2_fn, [f"rn{m}v1"]),
                    (f"rn{m}v", "v", rv_fn, [f"rn{m}s2"]),
                ])
                emit_transpose_2048(B(0, 0), B(0, 1), f"rw{m}")

                def st_fn(g, m=m):
                    g.dma_start(out=wup_hbm[bh, m].rearrange("(r p) d -> p r d", p=128),
                                in_=B(0, 1).rearrange("p (r d) -> p r d", d=512)
                                ).then_inc(c_st.s, 16)
                phase([(f"rn{m}st", "g", st_fn, [], ("st", 1))])

        def emit_chunk1_wload(bh):
            ld0 = c_ld.n

            def g_fn(g):
                for m, dst in enumerate((dw0t, dw1t, dw2t)):
                    g.dma_start(out=dst[:].rearrange("p (r d) -> p r d", d=512),
                                in_=wup_hbm[bh, m].rearrange("(r p) d -> p r d", p=128)
                                ).then_inc(c_ld.s, 16)
            phase([("c1w", "g", g_fn, [], ("ld", 3))])
            return ld0 + 48

        if any(t.startswith("tap:") for t in dbg):
            def ms_fn(g):
                return g.memset(dB[:], 0)
            def ms2_fn(v):
                v.memset(dG[:], 0)
                v.memset(dfr[:], 0)
                v.memset(dnrm[:], 0)
                return v.memset(drow[:], 0)
            phase([("msB", "g", ms_fn, []), ("msG", "v", ms2_fn, [])])

        def tap(name):
            if f"tap:{name}" not in dbg:
                return
            tdb = dram(f"tap_{name}", [128, 11 * 4096], BF16, "ExternalOutput")
            tdg = dram(f"tapg_{name}", [128, 3 * 2048], F32, "ExternalOutput")
            tdr = dram(f"tapr_{name}", [128, 64], F32, "ExternalOutput")

            def g_fn(g):
                g.wait_ge(c_ld.s, c_ld.n)
                g.dma_start(out=tdb, in_=dB[:]).then_inc(c_st.s, 16)
                g.dma_start(out=tdg, in_=dG[:]).then_inc(c_st.s, 16)
                g.dma_start(out=tdr[0:1, 0:16].rearrange("a (x c) -> a x c", x=1),
                            in_=drow[:, 0:16].rearrange("a (x c) -> a x c", x=1)).then_inc(c_st.s, 16)
                g.dma_start(out=tdr[:, 16:32], in_=dfr[:]).then_inc(c_st.s, 16)
                g.dma_start(out=tdr[:, 32:48], in_=dnrm[:]).then_inc(c_st.s, 16)
            phase([(f"tp{name}", "g", g_fn, [], ("st", 5))])

        heads = range(1 if "d1bh" in dbg else NH)
        bats = range(1 if "d1bh" in dbg else NB)
        for h in heads:
            emit_head_prep(h)
            for b in bats:
                ldt = emit_loads(b, h, 0, with_kv=True)
                tap("ld0")
                emit_norm_prologue(0, 3, "q", ldt)
                tap("q")
                emit_norm_prologue(1, 4, "k", ldt)
                tap("k")

                def vsilu_fn(s):
                    return s.activation(B(2), B(2), AF.Silu)
                phase([("vsi", "s", vsilu_fn, [("ld", ldt)])])
                emit_forward(b, h, 0)
                tap("fwd")
                emit_rms_store(b, h, 0, ldt)
                tap("rms")
                emit_tok2fm(B(2), B(5), "vT")
                emit_fm2tok(B(4), B(0), "knT")
                def klf(v):
                    ii = None
                    for ct in range(8):
                        v.tensor_scalar_mul(B(1)[:, ct * 512:(ct + 1) * 512],
                                            B(0)[:, ct * 512:(ct + 1) * 512],
                                            lrcol(b, ct, h, 0))
                        ii = v.tensor_scalar_mul(B(0)[:, ct * 512:(ct + 1) * 512],
                                                 B(0)[:, ct * 512:(ct + 1) * 512],
                                                 lrcol(b, ct, h, 2))
                    return ii
                phase([("kl", "v", klf, [])])
                tap("kl")
                emit_backward(b, h)
                tap("bw")
                emit_fm2tok(B(9), B(4), "hl1",
                            scale_fns=lambda ct: lrcol(b, ct, h, 1))
                emit_fm2tok(B(6), B(8), "dgt")
                emit_fm2tok(B(10), B(3), "dht")
                emit_dw(b, h, "dw1", 2, 4, 1)
                emit_dw(b, h, "dw0", 8, 1, 0)
                emit_dw(b, h, "dw2", 3, 0, 2)
                tap("dw")
                bh = b * NH + h
                emit_x_prep(bh)
                tap("x")
                emit_ns(bh)
                tap("ns")
                emit_renorm_store(bh)
        for h in heads:
            for b in bats:
                bh = b * NH + h
                wldt = emit_chunk1_wload(bh)
                ldt = emit_loads(b, h, 1, with_kv=False)
                emit_norm_prologue(0, 3, "q1", ldt)
                emit_forward(b, h, 1, ldw=wldt)
                emit_rms_store(b, h, 1, ldt)
        phase([], extra={"g": lambda g: (g.wait_ge(c_ld.s, c_ld.n),
                                         g.wait_ge(c_st.s, c_st.n))})

    if upto == 'd':
        return nc, out

    with (
        nc.sbuf_tensor("e_spacer", [128, 8192], BF16) as _e_spacer,
        nc.sbuf_tensor("e_ttt", [128, 16 * CH], BF16) as e_ttt,
        nc.sbuf_tensor("e_aq", [128, 8 * CH], BF16) as e_aq,
        nc.sbuf_tensor("e_ak", [128, 8 * 2 * CH], BF16) as e_ak,
        nc.sbuf_tensor("e_v", [128, 16 * 8 * 128], BF16) as e_v,
        nc.sbuf_tensor("e_at", [128, 16 * CH], BF16) as e_at,
        nc.sbuf_tensor("e_P", [128, 2 * 1280], BF16) as e_P,
        nc.sbuf_tensor("e_PT", [128, 2 * 1280], BF16) as e_PT,
        nc.sbuf_tensor("e_r", [128, 8], F32) as e_r,
        nc.sbuf_tensor("e_wo", [128, 2 * 2048], BF16) as e_wo,
        nc.sbuf_tensor("e_stage", [128, 2 * CH], BF16) as e_stage,
        nc.psum_tensor("e_psS", [128, 2 * 1536], F32) as e_psS,
        nc.psum_tensor("e_psT", [128, 1024], BF16) as e_psT,
        nc.psum_tensor("e_psO", [128, 512], F32) as e_psO,
    ):
        for b in range(NB):
            for n in range(2):
                kw = CH if n == 0 else 2 * CH
                nkt = kw // 128
                w0tok = b * L + (0 if n == 0 else (n - 1) * CH)
                for g_ in range(2):
                    ld0 = c_ld.n

                    def ld_fn(gg, b=b, n=n, g_=g_, kw=kw, w0tok=w0tok):
                        qc = b * L + n * CH
                        gg.dma_start(
                            out=e_aq[:].rearrange("p (a c) -> p a c", c=CH),
                            in_=qkvT_hbm[g_ * CH:(g_ + 1) * CH, qc:qc + CH]
                            .rearrange("(a p) c -> p a c", p=128)).then_inc(c_ld.s, 16)
                        gg.dma_start(
                            out=e_ak[:, 0:8 * kw].rearrange("p (a c) -> p a c", c=kw),
                            in_=qkvT_hbm[D + g_ * CH: D + (g_ + 1) * CH,
                                         w0tok:w0tok + kw]
                            .rearrange("(a p) c -> p a c", p=128)).then_inc(c_ld.s, 16)
                        gg.dma_start(
                            out=e_v[:, 0:nkt * 8 * 128].rearrange(
                                "p (a hh c) -> p a hh c", hh=8, c=128),
                            in_=av_hbm[w0tok:w0tok + kw,
                                       g_ * CH:(g_ + 1) * CH]
                            .rearrange("(a p) (hh c) -> p a hh c", p=128, c=128)
                            ).then_inc(c_ld.s, 16)

                    phase([("el", "g", ld_fn, [], ("ld", 3))])

                    def sc_fn(v, b=b, n=n, g_=g_, kw=kw):
                        ii = None
                        for hl in range(8):
                            ft = g_ * 8 + hl
                            ii = v.tensor_scalar(
                                e_aq[:, hl * CH:(hl + 1) * CH],
                                e_aq[:, hl * CH:(hl + 1) * CH],
                                qks[:, ft * 4 + 0: ft * 4 + 1],
                                qks[:, ft * 4 + 2: ft * 4 + 3],
                                op0=ALU.mult, op1=ALU.add)
                        for hl in range(8):
                            ft = g_ * 8 + hl
                            ii = v.tensor_scalar(
                                e_ak[:, hl * kw:(hl + 1) * kw],
                                e_ak[:, hl * kw:(hl + 1) * kw],
                                qks[:, ft * 4 + 1: ft * 4 + 2],
                                qks[:, ft * 4 + 3: ft * 4 + 4],
                                op0=ALU.mult, op1=ALU.add)
                        return ii

                    if "dbg_eakraw" in dbg and b == 0 and n == 0 and g_ == 0:
                        eakr_dbg = dram("eakr_dbg", [128, 8 * 2 * CH], BF16, "ExternalOutput")

                        def eakr_fn(gg):
                            gg.wait_ge(c_ld.s, c_ld.n)
                            gg.dma_start(out=eakr_dbg, in_=e_ak[:]).then_inc(c_st.s, 16)
                        phase([("eakrd", "g", eakr_fn, [], ("st", 1))])

                    phase([("esc", "v", sc_fn, [("ld", ld0 + 48)])])
                    if "dbg_eak" in dbg and b == 0 and n == 0 and g_ == 0:
                        eak_dbg = dram("eak_dbg", [128, 8 * 2 * CH], BF16, "ExternalOutput")
                        eaq_dbg = dram("eaq_dbg", [128, 8 * CH], BF16, "ExternalOutput")

                        def eak_fn(gg):
                            gg.dma_start(out=eak_dbg, in_=e_ak[:]).then_inc(c_st.s, 16)
                            gg.dma_start(out=eaq_dbg, in_=e_aq[:]).then_inc(c_st.s, 16)
                        phase([("eakd", "g", eak_fn, [], ("st", 2))])

                    units = []
                    uid = [0]
                    unw = {}
                    for hl in range(8):
                        for t in range(8):
                            u = uid[0]
                            uid[0] += 1
                            sp = u % 2
                            if n == 0:
                                ktlo, kthi = 0, t
                            else:
                                ktlo, kthi = t, t + 8
                            span = (kthi - ktlo + 1) * 128
                            tagp = f"u{u}"

                            def s_mm(tt, hl=hl, t=t, sp=sp, ktlo=ktlo, span=span, kw=kw):
                                ii = None
                                off = hl * kw + ktlo * 128
                                for c0 in range(0, span, 512):
                                    cw = min(512, span - c0)
                                    ii = tt.matmul(
                                        e_psS[:, sp * 1536 + c0: sp * 1536 + c0 + cw],
                                        e_aq[:, hl * CH + t * 128: hl * CH + (t + 1) * 128],
                                        e_ak[:, off + c0: off + c0 + cw],
                                        start=True, stop=True)
                                return ii

                            def s_mask(v, sp=sp, span=span, n=n, t=t):
                                if n == 1:
                                    v.tensor_add(e_psS[:, sp * 1536: sp * 1536 + 128],
                                                 e_psS[:, sp * 1536: sp * 1536 + 128],
                                                 maska[:])
                                ii = v.tensor_add(
                                    e_psS[:, sp * 1536 + span - 128: sp * 1536 + span],
                                    e_psS[:, sp * 1536 + span - 128: sp * 1536 + span],
                                    maskc[:])
                                return ii

                            def s_max(v, sp=sp, span=span):
                                return v.reduce_max(e_r[:, 0:1],
                                                    e_psS[:, sp * 1536: sp * 1536 + span],
                                                    axis=AX.X, negate=True)

                            def s_exp(sc, sp=sp, span=span, u=u):
                                pp = u % 2
                                return sc.activation(e_P[:, pp * 1280: pp * 1280 + span],
                                                     e_psS[:, sp * 1536: sp * 1536 + span],
                                                     AF.Exp, bias=e_r[:, 0:1],
                                                     accum_out=e_r[:, 1:2])

                            def s_den(sc):
                                sc.activation(e_r[:, 2:3], e_r[:, 1:2], AF.Ln)
                                return sc.activation(e_r[:, 2:3], e_r[:, 2:3], AF.Exp,
                                                     scale=cb(-1.0))

                            def s_pn(v, u=u, span=span):
                                pp = u % 2
                                return v.tensor_scalar_mul(
                                    e_P[:, pp * 1280: pp * 1280 + span],
                                    e_P[:, pp * 1280: pp * 1280 + span],
                                    e_r[:, 2:3])

                            nw = (span // 128 + 3) // 4
                            unw[u] = nw
                            sdeps = [f"u{u-2}exp"] if u >= 2 else []
                            units.append((f"{tagp}mm", "t", s_mm, sdeps))
                            units.append((f"{tagp}ms", "v", s_mask, [f"{tagp}mm"]))
                            units.append((f"{tagp}mx", "v", s_max, [f"{tagp}ms"]
                                          + ([f"u{u-1}exp"] if u >= 1 else [])))
                            units.append((f"{tagp}exp", "s", s_exp, [f"{tagp}mx"]
                                          + ([f"u{u-2}pt{unw[u-2]-1}"] if u >= 2 else [])))
                            units.append((f"{tagp}den", "s", s_den,
                                          [f"u{u-1}pn"] if u >= 1 else []))
                            units.append((f"{tagp}pn", "v", s_pn, [f"{tagp}den"]))
                            for w in range(nw):
                                j0 = w * 4
                                jn = min(4, span // 128 - j0)

                                def pt_mm(tt, u=u, w=w, j0=j0, jn=jn):
                                    pp = u % 2
                                    wp = w % 2
                                    ii = None
                                    for j in range(jn):
                                        ii = tt.transpose(
                                            e_psT[:, wp * 512 + j * 128: wp * 512 + (j + 1) * 128],
                                            e_P[:, pp * 1280 + (j0 + j) * 128:
                                                pp * 1280 + (j0 + j + 1) * 128],
                                            ident[:])
                                    return ii

                                def pt_ev(v, u=u, w=w, j0=j0, jn=jn):
                                    pp = u % 2
                                    wp = w % 2
                                    return v.tensor_copy(
                                        e_PT[:, pp * 1280 + j0 * 128:
                                             pp * 1280 + (j0 + jn) * 128],
                                        e_psT[:, wp * 512: wp * 512 + jn * 128])

                                def pv_mm(tt, u=u, w=w, j0=j0, jn=jn, hl=hl,
                                          ktlo=ktlo, nw=nw):
                                    pp = u % 2
                                    ii = None
                                    for j in range(jn):
                                        kt = ktlo + j0 + j
                                        ii = tt.matmul(
                                            e_psO[:, 0:128],
                                            e_v[:, (kt * 8 + hl) * 128:
                                                (kt * 8 + hl + 1) * 128],
                                            e_PT[:, pp * 1280 + (j0 + j) * 128:
                                                 pp * 1280 + (j0 + j + 1) * 128],
                                            start=(w == 0 and j == 0),
                                            stop=(w == nw - 1 and j == jn - 1))
                                    return ii
                                units.append((f"{tagp}pt{w}", "t", pt_mm,
                                              [f"{tagp}pn"] + ([f"{tagp}pe{w-2}"] if w >= 2 else [])))
                                pedeps = [f"{tagp}pt{w}"]
                                if w == 0 and u >= 2:
                                    pedeps.append(f"u{u-2}pv{unw[u-2]-1}")
                                units.append((f"{tagp}pe{w}", "v", pt_ev, pedeps))
                                pvdeps = [f"{tagp}pe{w}"]
                                if w == 0 and u >= 1:
                                    pvdeps.append(f"u{u-1}oev")
                                units.append((f"{tagp}pv{w}", "t", pv_mm, pvdeps))

                            def o_ev(v, hl=hl, t=t, g_=g_):
                                return v.tensor_copy(
                                    e_at[:, (g_ * 8 + hl) * CH + t * 128:
                                         (g_ * 8 + hl) * CH + (t + 1) * 128],
                                    e_psO[:, 0:128])

                            units.append((f"{tagp}oev", "v", o_ev, [f"{tagp}pv{nw-1}"]))
                    phase(units)
                    if "dbg_eP" in dbg and b == 0 and n == 0 and g_ == 0:
                        ep_dbg = dram("ep_dbg", [128, 1280], BF16, "ExternalOutput")
                        er_dbg = dram("er_dbg", [128, 8], F32, "ExternalOutput")

                        es_dbg = dram("es_dbg", [128, 1024], F32, "ExternalOutput")

                        def ev_fn(v):
                            return v.tensor_copy(e_stage[:].bitcast(F32)[:, 0:1024],
                                                 e_psS[:, 1536:2560])

                        def ep_fn(gg):
                            gg.dma_start(out=ep_dbg, in_=e_P[:, 1280:2560]).then_inc(c_st.s, 16)
                            gg.dma_start(out=er_dbg, in_=e_r[:]).then_inc(c_st.s, 16)
                            gg.dma_start(out=es_dbg, in_=e_stage[:].bitcast(F32)[:, 0:1024]
                                         ).then_inc(c_st.s, 16)
                        phase([("epv", "v", ev_fn, []),
                               ("epd", "g", ep_fn, ["epv"], ("st", 3))])

                if "dbg_eat" in dbg and b == 0:
                    eat_dbg = dram(f"eat_dbg{n}", [128, 16 * CH], BF16, "ExternalOutput")

                    def eat_fn(gg, eat_dbg=eat_dbg):
                        gg.dma_start(out=eat_dbg, in_=e_at[:]).then_inc(c_st.s, 16)
                    phase([("eatd", "g", eat_fn, [], ("st", 1))])

                ld1 = c_ld.n

                def tld_fn(gg, b=b, n=n):
                    cc = b * L + n * CH
                    gg.dma_start(out=e_ttt[:].rearrange("p (a c) -> p a c", c=CH),
                                 in_=tttT_hbm[:, cc:cc + CH]
                                 .rearrange("(a p) c -> p a c", p=128)
                                 ).then_inc(c_ld.s, 16)

                def add_fn(v):
                    ii = None
                    for ft in range(16):
                        ii = v.tensor_add(e_at[:, ft * CH:(ft + 1) * CH],
                                          e_at[:, ft * CH:(ft + 1) * CH],
                                          e_ttt[:, ft * CH:(ft + 1) * CH])
                    return ii

                phase([("etl", "g", tld_fn, [], ("ld", 1)),
                       ("ead", "v", add_fn, [("ld", ld1 + 16)])])

                wunits = []
                for dblk in range(16):
                    wp = dblk % 2

                    def wl_fn(gg, dblk=dblk, wp=wp):
                        gg.dma_start(
                            out=e_wo[:, wp * 2048:(wp + 1) * 2048].rearrange(
                                "p (a c) -> p a c", c=128),
                            in_=wob_hbm[:, dblk * 128:(dblk + 1) * 128]
                            .rearrange("(a p) c -> p a c", p=128)).then_inc(c_ld.s, 16)

                    def wm_fn(tt, dblk=dblk, wp=wp):
                        ii = None
                        for ht in range(16):
                            for cq in range(2):
                                ii = tt.matmul(
                                    e_psS[:, wp * 1536 + cq * 512: wp * 1536 + (cq + 1) * 512],
                                    e_wo[:, wp * 2048 + ht * 128: wp * 2048 + (ht + 1) * 128],
                                    e_at[:, ht * CH + cq * 512: ht * CH + (cq + 1) * 512],
                                    start=(ht == 0), stop=(ht == 15))
                        return ii

                    def we_fn(v, dblk=dblk, wp=wp):
                        return v.tensor_copy(e_stage[:, wp * CH:(wp + 1) * CH],
                                             e_psS[:, wp * 1536: wp * 1536 + CH])

                    def ws_fn(gg, dblk=dblk, wp=wp, b=b, n=n):
                        cc = b * L + n * CH
                        gg.dma_start(out=sT_hbm[dblk * 128:(dblk + 1) * 128, cc:cc + CH],
                                     in_=e_stage[:, wp * CH:(wp + 1) * CH]
                                     ).then_inc(c_st.s, 16)

                    wunits.append((f"w{dblk}l", "g", wl_fn,
                                   [f"w{dblk-2}m"] if dblk >= 2 else [], ("ld", 1)))
                    wunits.append((f"w{dblk}m", "t", wm_fn,
                                   [f"w{dblk}l"] + ([f"w{dblk-2}e"] if dblk >= 2 else [])))
                    wunits.append((f"w{dblk}e", "v", we_fn,
                                   [f"w{dblk}m"] + ([f"w{dblk-2}s"] if dblk >= 2 else [])))
                    wunits.append((f"w{dblk}s", "g", ws_fn, [f"w{dblk}e"], ("st", 1)))
                phase(wunits)
        phase([], extra={"g": lambda g: (g.wait_ge(c_ld.s, c_ld.n),
                                         g.wait_ge(c_st.s, c_st.n))})

    if upto == 'e':
        return nc, out

    with (
        nc.sbuf_tensor("c2_wd", [128, 64 * 1024], BF16) as c2_wd,
        nc.sbuf_tensor("c2_h", [128, 2 * 64 * 128], BF16) as c2_h,
        nc.sbuf_tensor("c2_x", [128, 2 * 1024], F32) as c2_x,
        nc.sbuf_tensor("c2_s", [128, 2 * 1024], BF16) as c2_s,
        nc.sbuf_tensor("c2_g", [128, 1024], F32) as c2_g,
        nc.sbuf_tensor("c2_o", [128, 2 * 1024], F32) as c2_o,
        nc.psum_tensor("c2_ps", [128, 4 * 512], F32) as c2_ps,
    ):
        for hf in range(2):
            wl0 = c_ld.n

            def wdl_fn(gg, hf=hf):
                pass

            def wdl_sync(ss, hf=hf):
                for isl in range(64):
                    ss.dma_start(out=c2_wd[:, isl * 1024:(isl + 1) * 1024],
                                 in_=wdb_hbm[hf * 1024:(hf + 1) * 1024,
                                             isl * 128:(isl + 1) * 128],
                                 transpose=True).then_inc(c_ld.s, 16)

            with nc.Block() as blk:
                blk.sync(wdl_sync)
            c_ld.n += 64 * 16
            wdtgt = c_ld.n

            units = []
            for ts_ in range(32):
                pp = ts_ % 2
                tok0 = ts_ * 128
                b = tok0 // L
                trow = tok0 % L
                tag = f"c2_{hf}_{ts_}"

                def hl_fn(gg, pp=pp, tok0=tok0, hf=hf, ts_=ts_):
                    gg.dma_start(out=c2_h[:, pp * 8192:(pp + 1) * 8192]
                                 .rearrange("p (a c) -> p a c", c=128),
                                 in_=h_hbm[:, tok0:tok0 + 128]
                                 .rearrange("(a p) c -> p a c", p=128)
                                 ).then_inc(c_ld.s, 16)
                    gg.dma_start(out=c2_x[:, pp * 1024:(pp + 1) * 1024],
                                 in_=x_in[tok0 // L, tok0 % L:tok0 % L + 128,
                                          hf * 1024:(hf + 1) * 1024]
                                 ).then_inc(c_ld.s, 16)

                def sl_fn(ss, pp=pp, tok0=tok0, hf=hf):
                    ss.dma_start(out=c2_s[:, pp * 1024:(pp + 1) * 1024],
                                 in_=sT_hbm[hf * 1024:(hf + 1) * 1024,
                                            tok0:tok0 + 128],
                                 transpose=True).then_inc(c_ld.s, 16)

                def mm_fn(tt, pp=pp, hf=hf, ts_=ts_):
                    ii = None
                    for hc in range(2):
                        for a in range(64):
                            ii = tt.matmul(
                                c2_ps[:, (pp * 2 + hc) * 512:(pp * 2 + hc + 1) * 512],
                                c2_h[:, pp * 8192 + a * 128: pp * 8192 + (a + 1) * 128],
                                c2_wd[:, a * 1024 + hc * 512: a * 1024 + hc * 512 + 512],
                                start=(a == 0), stop=(a == 63))
                    return ii

                def ep_fn(sc, pp=pp):
                    ii = None
                    for hc in range(2):
                        ii = sc.activation(c2_g[:, hc * 512:(hc + 1) * 512],
                                           c2_ps[:, (pp * 2 + hc) * 512:
                                                 (pp * 2 + hc + 1) * 512],
                                           AF.Silu)
                    return ii

                def cmb_fn(v, pp=pp):
                    v.tensor_mul(c2_o[:, pp * 1024:(pp + 1) * 1024],
                                 c2_g[:], c2_s[:, pp * 1024:(pp + 1) * 1024])
                    return v.tensor_add(c2_o[:, pp * 1024:(pp + 1) * 1024],
                                        c2_o[:, pp * 1024:(pp + 1) * 1024],
                                        c2_x[:, pp * 1024:(pp + 1) * 1024])

                def st_fn(gg, pp=pp, tok0=tok0, hf=hf):
                    gg.dma_start(out=out[tok0 // L, tok0 % L:tok0 % L + 128,
                                         hf * 1024:(hf + 1) * 1024],
                                 in_=c2_o[:, pp * 1024:(pp + 1) * 1024]
                                 ).then_inc(c_st.s, 16)

                ldeps = [f"c2_{hf}_{ts_-2}mm"] if ts_ >= 2 else []
                units.append((f"{tag}hl", "g", hl_fn, ldeps, ("ld", 2)))
                units.append((f"{tag}sl", "s", sl_fn,
                              [f"c2_{hf}_{ts_-2}cmb"] if ts_ >= 2 else [], ("ld", 1)))
                mmdeps = [f"{tag}hl", ("ld", wdtgt)]
                if ts_ >= 2:
                    mmdeps.append(f"c2_{hf}_{ts_-2}ep")
                units.append((f"{tag}mm", "t", mm_fn, mmdeps))
                epdeps = [f"{tag}mm"]
                if ts_ >= 1:
                    epdeps.append(f"c2_{hf}_{ts_-1}cmb")
                units.append((f"{tag}ep", "s", ep_fn, epdeps))
                cmbdeps = [f"{tag}ep", f"{tag}sl"]
                if ts_ >= 2:
                    cmbdeps.append(f"c2_{hf}_{ts_-2}st")
                units.append((f"{tag}cmb", "v", cmb_fn, cmbdeps))
                units.append((f"{tag}st", "g", st_fn, [f"{tag}cmb"], ("st", 1)))
            phase(units)
        phase([], extra={"g": lambda g: (g.wait_ge(c_ld.s, c_ld.n),
                                         g.wait_ge(c_st.s, c_st.n)),
                         "s": lambda sc: sc.wait_ge(c_st.s, c_st.n)})

    if "dbg_small" in dbg:
        lr_dbg = dram("lr_dbg", [128, 32 * 12], F32, "ExternalOutput")
        scl_dbg = dram("scl_dbg", [4, TOK], F32, "ExternalOutput")
        with nc.Block() as blk:
            @blk.gpsimd
            def _(g):
                g.dma_start(out=lr_dbg, in_=lr_tok[:]).then_inc(sem("dbgs"), 16)
                g.dma_start(out=scl_dbg, in_=scl4[:]).then_inc(sem("dbgs"), 16)
                g.wait_ge(sem("dbgs"), 32)

    return nc, out



def _host_consts():
    ident = np.eye(128, dtype=ml_dtypes.bfloat16)
    ii = np.arange(128)
    maskc = np.where(ii[None, :] <= ii[:, None], 0.0, NEG).astype(np.float32)
    maska = np.where(ii[None, :] >= ii[:, None], 0.0, NEG).astype(np.float32)
    nsdiag = np.stack([(np.eye(128) * (b / c)).astype(ml_dtypes.bfloat16)
                       for (a, b, c) in NS_COEFFS])
    return ident, maskc, maska, nsdiag


def _make_in_map(core, inputs):
    x = inputs["L0_attn_x"]
    ident, maskc, maska, nsdiag = _host_consts()
    wlr = inputs["L0_attn_Wlr"]
    ws = inputs["L0_attn_Ws"]
    wsp = np.zeros((D, 36), np.float32)
    wsp[:, 0:12] = wlr
    wsp[:, 32:36] = ws
    qs = inputs["L0_attn_qk_scale"]
    qo = inputs["L0_attn_qk_offset"]
    qkso = np.stack([qs[:, 0] * INV_SQRT_AHD, qs[:, 1],
                     qo[:, 0] * INV_SQRT_AHD, qo[:, 1]], axis=1).astype(np.float32)
    return {
        "x": np.ascontiguousarray(x[core * NB:(core + 1) * NB]),
        "wqkv": inputs["L0_attn_Wqkv"],
        "wsp": wsp,
        "bs": inputs["L0_attn_bs"],
        "wo": inputs["L0_attn_Wo"],
        "w0": inputs["L0_attn_w0"],
        "w1": inputs["L0_attn_w1"],
        "w2": inputs["L0_attn_w2"],
        "onormg": inputs["L0_attn_o_norm_g"],
        "qkso": qkso,
        "gw": inputs["L0_ffn_gate_w"],
        "gsc": np.ascontiguousarray(inputs["L0_ffn_gate_scale"].reshape(IF // 128, 128).T),
        "uw": inputs["L0_ffn_up_w"],
        "usc": np.ascontiguousarray(inputs["L0_ffn_up_scale"].reshape(IF // 128, 128).T),
        "dw": inputs["L0_ffn_down_w"],
        "dsc": inputs["L0_ffn_down_scale"],
        "ident": ident,
        "ones": np.ones((128, 128), dtype=ml_dtypes.bfloat16),
        "maskc": maskc,
        "maska": maska,
        "nsdiag": nsdiag,
        "cvec": np.tile(np.asarray(CONSTS, np.float32)[None, :], (128, 1)),
    }


def kernel(**inputs):
    nc, _ = build()
    in_maps = [_make_in_map(c, inputs) for c in range(8)]
    res = run_bass_kernel_spmd(nc, in_maps, core_ids=list(range(8)))
    return np.concatenate([r["out"] for r in res.results], axis=0)
